# version 02
# baseline (speedup 1.0000x reference)
import numpy as np
import ml_dtypes
import concourse.bass as bass
import concourse.mybir as mybir
from concourse.bass_utils import run_bass_kernel_spmd

BFNP = ml_dtypes.bfloat16
F32 = mybir.dt.float32
BF16 = mybir.dt.bfloat16
AF = mybir.ActivationFunctionType
ALU = mybir.AluOpType
AX = mybir.AxisListType

D = 2048
DC = 16
T = 1024
C = 3072
CT = 24
OT0 = 16
H = 16
F = 8192
FC = 64
J = 20
QB = 2
NCORES = 8
SCALE = 1.0 / float(np.sqrt(np.float32(128.0)))
LN_EPS = 1e-5
RMS_EPS = 1e-8
PAD_BIAS = -88.0

AR1_COLS = 65536
def xnf_col(c, t=0):   return c * 3072 + t * 128
def wqtr_col(buf, k=0): return 49152 + buf * 8192 + k * 512
def p_col(hc, q=0):    return hc * 1024 + q
def ctx_col(h, q=0):   return h * 1024 + q
def kbuf_col(pb, hl):  return 16384 + pb * 12288 + hl * 3072
def vbuf_col(pb, tokt=0): return 16384 + pb * 12288 + 6144 + tokt * 256
def wo_col(k):         return 16384 + k * 2048

def xslot_col(s):      return s * 2048
AR2F_SCR_A = 4096
AR2F_SCR_B = 5120
def gring_col(s):      return 4096 + s * 512

def xnslot_col(s):     return s * 2048
def pwslot_col(s):     return 4096 + s * 2048
def wstage_col(s):     return s * 2048
def wdslice_col(s):    return s * 512
MASKS_COL = 0
def ptil_col(s):       return 4096 + s * 512
def qring_col(s):      return 6144 + s * 512

ENGS = ("T", "V", "S", "Y", "G")


NSLOT = 12
DMA_SEMS = [f"{q}{i}" for q in ("Y", "S", "G") for i in range(NSLOT)]
ALL_SEMS = ["T", "V", "S"] + DMA_SEMS


class Prog:

    def __init__(self, nc):
        self.nc = nc
        self.thunks = {e: [] for e in ENGS}
        self.cnt = {s: 0 for s in ALL_SEMS}
        self.qrr = {"Y": 0, "S": 0, "G": 0}
        self.water = {}
        self.res = {}

    def _deps_read(self, iv):
        arena, lo, hi = iv
        deps = []
        for r in self.res.get(arena, []):
            if r["lo"] < hi and lo < r["hi"] and r["w"] is not None:
                deps.append(r["w"])
        return deps

    def _deps_write(self, iv):
        arena, lo, hi = iv
        deps = []
        keep = []
        for r in self.res.get(arena, []):
            if r["lo"] < hi and lo < r["hi"]:
                if r["w"] is not None:
                    deps.append(r["w"])
                deps.extend(r["r"])
                if r["lo"] < lo:
                    keep.append({"lo": r["lo"], "hi": lo, "w": r["w"], "r": list(r["r"])})
                if r["hi"] > hi:
                    keep.append({"lo": hi, "hi": r["hi"], "w": r["w"], "r": list(r["r"])})
            else:
                keep.append(r)
        self.res[arena] = keep
        return deps

    def _note_read(self, iv, tok):
        arena, lo, hi = iv
        for r in self.res.get(arena, []):
            if r["lo"] < hi and lo < r["hi"]:
                r["r"].append(tok)

    def op(self, eng, fn, reads=(), writes=(), dma=False):
        if dma:
            slot = self.qrr[eng]
            self.qrr[eng] = (slot + 1) % NSLOT
            sem = f"{eng}{slot}"
            step = 16
        else:
            sem = eng
            step = 1
        needs = {}
        for iv in reads:
            for (s, v) in self._deps_read(iv):
                if needs.get(s, 0) < v:
                    needs[s] = v
        for iv in writes:
            for (s, v) in self._deps_write(iv):
                if needs.get(s, 0) < v:
                    needs[s] = v
        if dma and self.cnt[sem] > 0:
            if needs.get(sem, 0) < self.cnt[sem]:
                needs[sem] = self.cnt[sem]
        waits = []
        for s, v in needs.items():
            if self.water.get((eng, s), 0) < v:
                self.water[(eng, s)] = v
                waits.append((s, v))
        self.cnt[sem] += step
        tok = (sem, self.cnt[sem])
        for iv in writes:
            arena, lo, hi = iv
            self.res.setdefault(arena, []).append(
                {"lo": lo, "hi": hi, "w": tok, "r": []})
        for iv in reads:
            self._note_read(iv, tok)

        def thunk(e, sems, waits=waits, fn=fn, sem=sem, step=step):
            for (s, v) in waits:
                e.wait_ge(sems[s], v)
            instr = fn(e)
            instr.then_inc(sems[sem], step)
        self.thunks[eng].append(thunk)

    def raw(self, eng, fn):
        self.thunks[eng].append(lambda e, sems, fn=fn: fn(e))

    def barrier_wait(self, eng, sem_targets):
        def thunk(e, sems, w=tuple(sem_targets)):
            for (s, v) in w:
                e.wait_ge(sems[s], v)
        self.thunks[eng].append(thunk)


def build_program(debug=False, phases="all", p2mode="full"):
    nc = bass.Bass("TRN2", target_bir_lowering=False)
    g = Prog(nc)

    x_tm = nc.dram_tensor("x_tm", [C, D], F32, kind="ExternalInput").ap()
    wq = nc.dram_tensor("wq", [D, D], F32, kind="ExternalInput").ap()
    wk = nc.dram_tensor("wk", [D, D], F32, kind="ExternalInput").ap()
    wv = nc.dram_tensor("wv", [D, D], F32, kind="ExternalInput").ap()
    wo = nc.dram_tensor("wo", [D, D], F32, kind="ExternalInput").ap()
    wgT = nc.dram_tensor("wgT", [D, F], F32, kind="ExternalInput").ap()
    wuT = nc.dram_tensor("wuT", [D, F], F32, kind="ExternalInput").ap()
    wdT = nc.dram_tensor("wdT", [F, D], F32, kind="ExternalInput").ap()
    masks_in = nc.dram_tensor("masks8", [128, 8 * 512], BF16, kind="ExternalInput").ap()
    padbias_in = nc.dram_tensor("padbias", [128, 2 * J], F32, kind="ExternalInput").ap()
    ident_in = nc.dram_tensor("ident", [128, 128], BF16, kind="ExternalInput").ap()
    onesc_in = nc.dram_tensor("onesc", [128, 1], BF16, kind="ExternalInput").ap()
    onesr_in = nc.dram_tensor("onesr", [1, 128], F32, kind="ExternalInput").ap()
    out_hbm = nc.dram_tensor("out", [T, D], F32, kind="ExternalOutput").ap()

    dbg = {}
    if debug:
        for nm, shape, dt in [("dbg_xnf", [128, 49152], BF16),
                              ("dbg_xq", [128, DC * T], BF16),
                              ("dbg_ctx", [128, 16384], BF16),
                              ("dbg_xntm", [128, 24 * 2048], BF16),
                              ("dbg_cols", [128, 24 * 8], F32),
                              ("dbg_bn", [128, 24 * 26], F32),
                              ("dbg_dn", [128, 2 * J], F32),
                              ("dbg_kf", [H * 6 * 128, 512], BF16),
                              ("dbg_vtm", [CT * 128, D], BF16),
                              ("dbg_qf", [H * QB * 128, 512], BF16),
                              ("dbg_g", [8 * 128, D], F32),
                              ("dbg_linv", [64, 512], F32)]:
            dbg[nm] = nc.dram_tensor(nm, shape, dt, kind="ExternalOutput").ap()

    d_kf = nc.dram_tensor("d_kf", [H, 6, 128, 512], BF16, kind="Internal").ap()
    d_vtm = nc.dram_tensor("d_vtm", [CT, 128, D], BF16, kind="Internal").ap()
    d_qf = nc.dram_tensor("d_qf", [H, QB, 128, 512], BF16, kind="Internal").ap()
    d_g = nc.dram_tensor("d_g", [8, 128, D], F32, kind="Internal").ap()
    d_wgs = nc.dram_tensor("d_wgs", [FC, 128, 2048], BF16, kind="Internal").ap()
    d_wus = nc.dram_tensor("d_wus", [FC, 128, 2048], BF16, kind="Internal").ap()
    d_wds = nc.dram_tensor("d_wds", [FC, 128, 2048], BF16, kind="Internal").ap()

    import contextlib
    with contextlib.ExitStack() as stk:
        def S_(name, shape, dt):
            return stk.enter_context(nc.sbuf_tensor(name, shape, dt))
        AR1 = S_("AR1", [128, AR1_COLS], BF16)
        AR2f = S_("AR2f", [128, 6144], F32)
        AR2b = S_("AR2b", [128, 8192], BF16)
        XQ = S_("XQ", [128, DC * T], BF16)
        ident = S_("identt", [128, 128], BF16)
        onesc = S_("onesct", [128, 1], BF16)
        onesr = S_("onesrt", [1, 128], F32)
        padb = S_("padbt", [128, 2 * J], F32)
        bn6 = S_("bn6", [128, 24], F32)
        bn2 = S_("bn2", [128, 2], F32)
        cols = S_("cols", [128, 8], F32)
        linv = S_("linv", [1, 512], F32)
        linvb = S_("linvb", [128, 512], F32)

        PS = [stk.enter_context(nc.psum_tensor(f"ps{i}", [128, 512], F32))
              for i in range(6)]
        PT = [stk.enter_context(nc.psum_tensor(f"pt{i}", [128, 128], BF16))
              for i in range(2)]

        sems = {}
        sem_ctx = contextlib.ExitStack()
        for s in ALL_SEMS:
            sems[s] = sem_ctx.enter_context(nc.semaphore("sem" + s))
        stk.enter_context(sem_ctx)
        block = stk.enter_context(nc.Block())

        A1 = "AR1"; A2F = "AR2f"; A2B = "AR2b"; AXQ = "XQ"
        PSR = "psum"
        DKF = "d_kf"; DVT = "d_vtm"; DQF = "d_qf"; DG = "d_g"
        DWG = "d_wgs"; DWU = "d_wus"; DWD = "d_wds"
        CST = "cst"
        CST_IDS = {"ident": 0, "onesc": 1, "onesr": 2, "padb": 3, "bn6": 4,
                   "bn2": 5, "cols_a": 6, "cols_b": 7, "cols_sd": 8,
                   "cols_m2": 9, "cols_rstd2": 10, "cols_amax": 11,
                   "cols_c1": 12, "cols_invs": 13, "linv": 14, "linvb": 15,
                   "masksep": 16}
        def cst(n):
            i = CST_IDS[n]
            return (CST, i, i + 1)
        def bank(i):
            return (PSR, i, i + 1)

        g.op("Y", lambda e: e.dma_start(out=ident[:], in_=ident_in),
             writes=[cst("ident")], dma=True)
        g.op("Y", lambda e: e.dma_start(out=onesc[:], in_=onesc_in),
             writes=[cst("onesc")], dma=True)
        g.op("Y", lambda e: e.dma_start(out=onesr[:], in_=onesr_in),
             writes=[cst("onesr")], dma=True)
        g.op("Y", lambda e: e.dma_start(out=padb[:], in_=padbias_in),
             writes=[cst("padb")], dma=True)
        g.op("V", lambda e: e.memset(cols[:], 0.0),
             writes=[cst("cols_a"), cst("cols_b"), cst("cols_sd"), cst("cols_m2"),
                     cst("cols_rstd2"), cst("cols_amax"), cst("cols_c1"), cst("cols_invs")])
        g.op("V", lambda e: e.memset(bn6[:], 0.0), writes=[cst("bn6")])
        g.op("V", lambda e: e.memset(bn2[:], 0.0), writes=[cst("bn2")])

        qkvo_plan = []
        def emit_qkvo_quarter_load(w_ap, a, buf):
            for k in range(DC):
                def fn(e, w_ap=w_ap, a=a, buf=buf, k=k):
                    return e.dma_start(
                        out=AR1[:, wqtr_col(buf, k):wqtr_col(buf, k) + 512],
                        in_=w_ap[k * 128:(k + 1) * 128, a * 512:(a + 1) * 512])
                g.op("G", fn, writes=[(A1, wqtr_col(buf, k), wqtr_col(buf, k) + 512)],
                     dma=True)

        def emit_pw_chunk(src_ap, dst_ap, dst_arena, idx, slot, colblock):
            lo = pwslot_col(slot)
            if colblock:
                for q4 in range(4):
                    def fn(e, src_ap=src_ap, idx=idx, lo=lo, q4=q4):
                        src = src_ap[q4 * 512:(q4 + 1) * 512,
                                     idx * 128:(idx + 1) * 128]
                        src = src.rearrange("(k p) h -> p k h", p=128)
                        dst = AR2b[:, lo + q4 * 512: lo + (q4 + 1) * 512]
                        dst = dst.rearrange("p (k h) -> p k h", k=4, h=128)
                        return e.dma_start(out=dst, in_=src)
                    g.op("G", fn, writes=[(A2B, lo + q4 * 512, lo + (q4 + 1) * 512)],
                         dma=True)
            else:
                def fn(e, src_ap=src_ap, idx=idx, lo=lo):
                    return e.dma_start(
                        out=AR2b[:, lo:lo + 2048],
                        in_=src_ap[idx * 128:(idx + 1) * 128, :])
                g.op("G", fn, writes=[(A2B, lo, lo + 2048)], dma=True)
            def fns(e, lo=lo):
                return e.sign(AR2b[:, lo:lo + 2048], AR2b[:, lo:lo + 2048])
            g.op("S", fns, reads=[(A2B, lo, lo + 2048)],
                 writes=[(A2B, lo, lo + 2048)])
            def fnw(e, dst_ap=dst_ap, idx=idx, lo=lo):
                return e.dma_start(out=dst_ap[idx], in_=AR2b[:, lo:lo + 2048])
            g.op("Y", fnw, reads=[(A2B, lo, lo + 2048)],
                 writes=[(dst_arena, idx, idx + 1)], dma=True)

        pw_jobs = []
        for idx in range(FC):
            pw_jobs.append((wgT, d_wgs, DWG, idx, True))
        for idx in range(FC):
            pw_jobs.append((wuT, d_wus, DWU, idx, True))
        for idx in range(FC):
            pw_jobs.append((wdT, d_wds, DWD, idx, False))
        pw_pos = 0
        pw_slot = [0]
        def emit_pw(n):
            nonlocal pw_pos
            for _ in range(n):
                if pw_pos >= len(pw_jobs):
                    return
                src_ap, dst_ap, dst_arena, idx, colblock = pw_jobs[pw_pos]
                emit_pw_chunk(src_ap, dst_ap, dst_arena, idx, pw_slot[0], colblock)
                pw_slot[0] ^= 1
                pw_pos += 1

        emit_qkvo_quarter_load(wk, 0, 0)
        emit_qkvo_quarter_load(wk, 1, 1)

        for t in range(CT):
            xs = t % 2
            xlo = xslot_col(xs)
            def fxl(e, t=t, xlo=xlo):
                return e.dma_start(out=AR2f[:, xlo:xlo + 2048],
                                  in_=x_tm[t * 128:(t + 1) * 128, :])
            g.op("Y", fxl, writes=[(A2F, xlo, xlo + 2048)], dma=True)
            for c4 in range(4):
                def fbs(e, xlo=xlo, c4=c4):
                    return e.bn_stats(bn6[:, c4 * 6:(c4 + 1) * 6],
                                      AR2f[:, xlo + c4 * 512: xlo + (c4 + 1) * 512])
                g.op("V", fbs, reads=[(A2F, xlo, xlo + 2048)], writes=[cst("bn6")])
            g.op("V", lambda e: e.bn_aggr(bn2[:], bn6[:]),
                 reads=[cst("bn6")], writes=[cst("bn2")])
            g.op("V", lambda e: e.tensor_scalar(cols[:, 2:3], bn2[:, 1:2],
                                                LN_EPS, None, ALU.add),
                 reads=[cst("bn2")], writes=[cst("cols_sd")])
            g.op("S", lambda e: e.activation(cols[:, 2:3], cols[:, 2:3], AF.Sqrt),
                 reads=[cst("cols_sd")], writes=[cst("cols_sd")])
            g.op("V", lambda e: e.reciprocal(cols[:, 0:1], cols[:, 2:3]),
                 reads=[cst("cols_sd")], writes=[cst("cols_a")])
            g.op("V", lambda e: e.tensor_scalar(cols[:, 1:2], bn2[:, 0:1],
                                                cols[:, 0:1], -1.0, ALU.mult,
                                                ALU.mult),
                 reads=[cst("bn2"), cst("cols_a")], writes=[cst("cols_b")])
            ns = t % 2
            nlo = xnslot_col(ns)
            def fap(e, xlo=xlo, nlo=nlo):
                return e.activation(AR2b[:, nlo:nlo + 2048],
                                    AR2f[:, xlo:xlo + 2048], AF.Identity,
                                    bias=cols[:, 1:2], scale=cols[:, 0:1])
            g.op("S", fap, reads=[(A2F, xlo, xlo + 2048), cst("cols_a"),
                                  cst("cols_b")],
                 writes=[(A2B, nlo, nlo + 2048)])
            if debug:
                def fdxn(e, t=t, nlo=nlo):
                    return e.dma_start(out=dbg["dbg_xntm"][:, t*2048:(t+1)*2048],
                                       in_=AR2b[:, nlo:nlo + 2048])
                g.op("Y", fdxn, reads=[(A2B, nlo, nlo + 2048)],
                     writes=[("dbgout", 3, 4)], dma=True)
                def fdcl(e, t=t):
                    return e.dma_start(out=dbg["dbg_cols"][:, t*8:(t+1)*8],
                                       in_=cols[:])
                g.op("Y", fdcl, reads=[cst("cols_a"), cst("cols_b"), cst("cols_sd")],
                     writes=[("dbgout", 4, 5)], dma=True)
                def fdb2(e, t=t):
                    return e.dma_start(out=dbg["dbg_bn"][:, t*26:t*26+2], in_=bn2[:])
                g.op("Y", fdb2, reads=[cst("bn2")], writes=[("dbgout", 5, 6)], dma=True)
                def fdb6(e, t=t):
                    return e.dma_start(out=dbg["dbg_bn"][:, t*26+2:t*26+26], in_=bn6[:])
                g.op("Y", fdb6, reads=[cst("bn6")], writes=[("dbgout", 6, 7)], dma=True)
            for c in range(DC):
                pb = c % 2
                def ftr(e, nlo=nlo, c=c, pb=pb):
                    return e.transpose(PT[pb][:], AR2b[:, nlo + c * 128: nlo + (c + 1) * 128],
                                       ident[:])
                g.op("T", ftr, reads=[(A2B, nlo, nlo + 2048), cst("ident")],
                     writes=[bank(6 + pb)])
                dst = xnf_col(c, t)
                def fev(e, pb=pb, dst=dst):
                    return e.tensor_copy(AR1[:, dst:dst + 128], PT[pb][:])
                g.op("V", fev, reads=[bank(6 + pb)], writes=[(A1, dst, dst + 128)])

            if t >= OT0:
                tq = t - OT0
                for c4 in range(4):
                    def fbs2(e, nlo=nlo, c4=c4):
                        return e.bn_stats(bn6[:, c4 * 6:(c4 + 1) * 6],
                                          AR2b[:, nlo + c4 * 512: nlo + (c4 + 1) * 512])
                    g.op("V", fbs2, reads=[(A2B, nlo, nlo + 2048)], writes=[cst("bn6")])
                g.op("V", lambda e: e.bn_aggr(bn2[:], bn6[:]),
                     reads=[cst("bn6")], writes=[cst("bn2")])
                g.op("V", lambda e: e.tensor_scalar(cols[:, 2:3], bn2[:, 1:2],
                                                    RMS_EPS, None, ALU.add),
                     reads=[cst("bn2")], writes=[cst("cols_sd")])
                g.op("S", lambda e: e.activation(cols[:, 2:3], cols[:, 2:3], AF.Sqrt),
                     reads=[cst("cols_sd")], writes=[cst("cols_sd")])
                g.op("V", lambda e: e.reciprocal(cols[:, 4:5], cols[:, 2:3]),
                     reads=[cst("cols_sd")], writes=[cst("cols_rstd2")])
                def ftt(e, nlo=nlo, xlo=xlo):
                    return e.tensor_scalar(AR2f[:, xlo:xlo + 2048],
                                           AR2b[:, nlo:nlo + 2048],
                                           bn2[:, 0:1], None, ALU.subtract)
                g.op("V", ftt, reads=[(A2B, nlo, nlo + 2048), cst("bn2")],
                     writes=[(A2F, xlo, xlo + 2048)])
                def fam(e, xlo=xlo):
                    return e.reduce_max(cols[:, 5:6], AR2f[:, xlo:xlo + 2048],
                                        axis=AX.X, apply_absolute_value=True)
                g.op("V", fam, reads=[(A2F, xlo, xlo + 2048)], writes=[cst("cols_amax")])
                g.op("V", lambda e: e.reciprocal(cols[:, 6:7], cols[:, 5:6]),
                     reads=[cst("cols_amax")], writes=[cst("cols_c1")])
                g.op("V", lambda e: e.tensor_scalar(cols[:, 6:7], cols[:, 6:7],
                                                    127.0, None, ALU.mult),
                     reads=[cst("cols_c1")], writes=[cst("cols_c1")])
                g.op("V", lambda e: e.tensor_scalar(cols[:, 7:8], cols[:, 5:6],
                                                    cols[:, 4:5], 1.0 / 127.0,
                                                    ALU.mult, ALU.mult),
                     reads=[cst("cols_amax"), cst("cols_rstd2")],
                     writes=[cst("cols_invs")])
                for hf in range(2):
                    tlo = xlo + hf * 1024
                    za = AR2F_SCR_A
                    zb = AR2F_SCR_B
                    def fz(e, tlo=tlo, za=za):
                        return e.tensor_scalar(AR2f[:, za:za + 1024],
                                               AR2f[:, tlo:tlo + 1024],
                                               cols[:, 6:7], 128.5,
                                               ALU.mult, ALU.add)
                    g.op("V", fz, reads=[(A2F, tlo, tlo + 1024), cst("cols_c1")],
                         writes=[(A2F, za, za + 1024)])
                    for kb in range(7, -1, -1):
                        def fb(e, za=za, zb=zb, kb=kb):
                            return e.tensor_scalar(AR2f[:, zb:zb + 1024],
                                                   AR2f[:, za:za + 1024],
                                                   float(2 ** kb), -float(2 ** kb),
                                                   ALU.is_ge, ALU.mult)
                        g.op("V", fb, reads=[(A2F, za, za + 1024)],
                             writes=[(A2F, zb, zb + 1024)])
                        def fs(e, za=za, zb=zb):
                            return e.tensor_tensor(AR2f[:, za:za + 1024],
                                                   AR2f[:, za:za + 1024],
                                                   AR2f[:, zb:zb + 1024], ALU.add)
                        g.op("V", fs, reads=[(A2F, za, za + 1024),
                                             (A2F, zb, zb + 1024)],
                             writes=[(A2F, za, za + 1024)])
                    def fz0(e, tlo=tlo, zb=zb):
                        return e.tensor_scalar(AR2f[:, zb:zb + 1024],
                                               AR2f[:, tlo:tlo + 1024],
                                               cols[:, 6:7], 128.5,
                                               ALU.mult, ALU.add)
                    g.op("V", fz0, reads=[(A2F, tlo, tlo + 1024), cst("cols_c1")],
                         writes=[(A2F, zb, zb + 1024)])
                    def fr(e, za=za, zb=zb):
                        return e.tensor_tensor(AR2f[:, zb:zb + 1024],
                                               AR2f[:, zb:zb + 1024],
                                               AR2f[:, za:za + 1024], ALU.subtract)
                    g.op("V", fr, reads=[(A2F, za, za + 1024), (A2F, zb, zb + 1024)],
                         writes=[(A2F, zb, zb + 1024)])
                    def fxq(e, zb=zb, nlo=nlo, hf=hf):
                        return e.tensor_scalar(AR2b[:, nlo + hf * 1024: nlo + (hf + 1) * 1024],
                                               AR2f[:, zb:zb + 1024],
                                               -128.0, cols[:, 7:8],
                                               ALU.add, ALU.mult)
                    g.op("V", fxq, reads=[(A2F, zb, zb + 1024), cst("cols_invs")],
                         writes=[(A2B, nlo + hf * 1024, nlo + (hf + 1) * 1024)])
                for c in range(DC):
                    pb = c % 2
                    def ftr2(e, nlo=nlo, c=c, pb=pb):
                        return e.transpose(PT[pb][:],
                                           AR2b[:, nlo + c * 128: nlo + (c + 1) * 128],
                                           ident[:])
                    g.op("T", ftr2, reads=[(A2B, nlo, nlo + 2048), cst("ident")],
                         writes=[bank(6 + pb)])
                    dst = c * T + tq * 128
                    def fev2(e, pb=pb, dst=dst):
                        return e.tensor_copy(XQ[:, dst:dst + 128], PT[pb][:])
                    g.op("V", fev2, reads=[bank(6 + pb)], writes=[(AXQ, dst, dst + 128)])

            emit_pw(8)

        if phases == "p0":
            for qt in range(8):
                def fz0(e, qt=qt):
                    return e.memset(AR2f[:, 0:2048], 0.0)
                g.op("V", fz0, writes=[(A2F, 0, 2048)])
                def fst0(e, qt=qt):
                    return e.dma_start(out=out_hbm[qt*128:(qt+1)*128, :],
                                       in_=AR2f[:, 0:2048])
                g.op("Y", fst0, reads=[(A2F, 0, 2048)],
                     writes=[("outhbm", qt, qt+1)], dma=True)
            if debug:
                for i6 in range(6):
                    def fdx0(e, i6=i6):
                        return e.dma_start(out=dbg["dbg_xnf"][:, i6*8192:(i6+1)*8192],
                                           in_=AR1[:, i6*8192:(i6+1)*8192])
                    g.op("Y", fdx0, reads=[(A1, i6*8192, (i6+1)*8192)],
                         writes=[("dbgout", 0, 1)], dma=True)
                for i2 in range(2):
                    def fdq0(e, i2=i2):
                        return e.dma_start(out=dbg["dbg_xq"][:, i2*8192:(i2+1)*8192],
                                           in_=XQ[:, i2*8192:(i2+1)*8192])
                    g.op("Y", fdq0, reads=[(AXQ, i2*8192, (i2+1)*8192)],
                         writes=[("dbgout", 1, 2)], dma=True)
            g.barrier_wait("G", [(f"G{i}", g.cnt[f"G{i}"]) for i in range(NSLOT)
                                 if g.cnt[f"G{i}"] > 0])
            g.barrier_wait("S", [(f"S{i}", g.cnt[f"S{i}"]) for i in range(NSLOT)
                                 if g.cnt[f"S{i}"] > 0])
        else:
            evr = [0]
            EVN = 2
            evac = S_("evac", [128, EVN * 512], BF16)
            EVA = "evac"

            def proj_common(kind, w_ap, quarter_src_done):
                for a in range(4):
                    buf = a % 2
                    if (kind, a) not in quarter_src_done:
                        emit_qkvo_quarter_load(w_ap, a, buf)
                    if kind == "k":
                        for mi in range(4):
                            h = a * 4 + mi
                            for tb in range(6):
                                pbk = (h * 6 + tb) % 2
                                for k in range(DC):
                                    def fmm(e, buf=buf, mi=mi, k=k, tb=tb, pbk=pbk):
                                        lhs = AR1[:, wqtr_col(buf, k) + mi * 128:
                                                  wqtr_col(buf, k) + (mi + 1) * 128]
                                        rhs = AR1[:, xnf_col(k) + tb * 512:
                                                  xnf_col(k) + (tb + 1) * 512]
                                        return e.matmul(PS[pbk][:], lhs, rhs,
                                                        start=(k == 0), stop=(k == DC - 1))
                                    g.op("T", fmm,
                                         reads=[(A1, wqtr_col(buf, k),
                                                 wqtr_col(buf, k) + 512),
                                                (A1, xnf_col(k) + tb * 512,
                                                 xnf_col(k) + (tb + 1) * 512)],
                                         writes=[bank(pbk)])
                                ev = (h * 6 + tb) % EVN
                                def fevk(e, pbk=pbk, ev=ev):
                                    return e.tensor_copy(evac[:, ev * 512:(ev + 1) * 512],
                                                         PS[pbk][:])
                                g.op("V", fevk, reads=[bank(pbk)],
                                     writes=[(EVA, ev * 512, (ev + 1) * 512)])
                                def fspk(e, h=h, tb=tb, ev=ev):
                                    return e.dma_start(out=d_kf[h, tb],
                                                       in_=evac[:, ev * 512:(ev + 1) * 512])
                                g.op("Y", fspk, reads=[(EVA, ev * 512, (ev + 1) * 512)],
                                     writes=[(DKF, h * 6 + tb, h * 6 + tb + 1)], dma=True)
                    elif kind == "v":
                        for tt in range(CT):
                            pbk = tt % 2
                            for k in range(DC):
                                def fmv(e, buf=buf, tt=tt, k=k, pbk=pbk):
                                    lhs = AR1[:, xnf_col(k) + tt * 128:
                                              xnf_col(k) + (tt + 1) * 128]
                                    rhs = AR1[:, wqtr_col(buf, k):wqtr_col(buf, k) + 512]
                                    return e.matmul(PS[pbk][:], lhs, rhs,
                                                    start=(k == 0), stop=(k == DC - 1))
                                g.op("T", fmv,
                                     reads=[(A1, wqtr_col(buf, k), wqtr_col(buf, k) + 512),
                                            (A1, xnf_col(k) + tt * 128,
                                             xnf_col(k) + (tt + 1) * 128)],
                                     writes=[bank(pbk)])
                            ev = tt % EVN
                            def fevv(e, pbk=pbk, ev=ev):
                                return e.tensor_copy(evac[:, ev * 512:(ev + 1) * 512],
                                                     PS[pbk][:])
                            g.op("V", fevv, reads=[bank(pbk)],
                                 writes=[(EVA, ev * 512, (ev + 1) * 512)])
                            def fspv(e, tt=tt, a=a, ev=ev):
                                return e.dma_start(
                                    out=d_vtm[tt, :, a * 512:(a + 1) * 512],
                                    in_=evac[:, ev * 512:(ev + 1) * 512])
                            g.op("Y", fspv, reads=[(EVA, ev * 512, (ev + 1) * 512)],
                                 writes=[(DVT, tt * 4 + a, tt * 4 + a + 1)], dma=True)
                    else:
                        for mi in range(4):
                            h = a * 4 + mi
                            for qb in range(QB):
                                tb = 4 + qb
                                pbk = (h * 2 + qb) % 2
                                for k in range(DC):
                                    def fmq(e, buf=buf, mi=mi, k=k, tb=tb, pbk=pbk):
                                        lhs = AR1[:, wqtr_col(buf, k) + mi * 128:
                                                  wqtr_col(buf, k) + (mi + 1) * 128]
                                        rhs = AR1[:, xnf_col(k) + tb * 512:
                                                  xnf_col(k) + (tb + 1) * 512]
                                        return e.matmul(PS[pbk][:], lhs, rhs,
                                                        start=(k == 0), stop=(k == DC - 1))
                                    g.op("T", fmq,
                                         reads=[(A1, wqtr_col(buf, k),
                                                 wqtr_col(buf, k) + 512),
                                                (A1, xnf_col(k) + tb * 512,
                                                 xnf_col(k) + (tb + 1) * 512)],
                                         writes=[bank(pbk)])
                                ev = (h * 2 + qb) % EVN
                                def fevq(e, pbk=pbk, ev=ev):
                                    return e.tensor_copy(evac[:, ev * 512:(ev + 1) * 512],
                                                         PS[pbk][:])
                                g.op("V", fevq, reads=[bank(pbk)],
                                     writes=[(EVA, ev * 512, (ev + 1) * 512)])
                                def fspq(e, h=h, qb=qb, ev=ev):
                                    return e.dma_start(out=d_qf[h, qb],
                                                       in_=evac[:, ev * 512:(ev + 1) * 512])
                                g.op("Y", fspq, reads=[(EVA, ev * 512, (ev + 1) * 512)],
                                     writes=[(DQF, h * 2 + qb, h * 2 + qb + 1)], dma=True)
                    emit_pw(10)

            proj_common("k", wk, {("k", 0), ("k", 1)})
            proj_common("v", wv, set())
            proj_common("q", wq, set())
            emit_pw(len(pw_jobs))

            if debug:
                for i6 in range(6):
                    def fdx(e, i6=i6):
                        return e.dma_start(out=dbg["dbg_xnf"][:, i6*8192:(i6+1)*8192],
                                           in_=AR1[:, i6*8192:(i6+1)*8192])
                    g.op("Y", fdx, reads=[(A1, i6*8192, (i6+1)*8192)],
                         writes=[("dbgout", 0, 1)], dma=True)
                for i2 in range(2):
                    def fdq(e, i2=i2):
                        return e.dma_start(out=dbg["dbg_xq"][:, i2*8192:(i2+1)*8192],
                                           in_=XQ[:, i2*8192:(i2+1)*8192])
                    g.op("Y", fdq, reads=[(AXQ, i2*8192, (i2+1)*8192)],
                         writes=[("dbgout", 1, 2)], dma=True)

            if phases == "p01":
                if debug:
                    def fdk(e):
                        return e.dma_start(out=dbg["dbg_kf"],
                                           in_=d_kf.rearrange("h t p n -> (h t p) n"))
                    g.op("Y", fdk, reads=[(DKF, 0, H * 6)], writes=[("dbgout", 7, 8)], dma=True)
                    def fdv(e):
                        return e.dma_start(out=dbg["dbg_vtm"],
                                           in_=d_vtm.rearrange("t p n -> (t p) n"))
                    g.op("Y", fdv, reads=[(DVT, 0, CT * 4)], writes=[("dbgout", 8, 9)], dma=True)
                    def fdqq(e):
                        return e.dma_start(out=dbg["dbg_qf"],
                                           in_=d_qf.rearrange("h q p n -> (h q p) n"))
                    g.op("Y", fdqq, reads=[(DQF, 0, H * QB)], writes=[("dbgout", 9, 10)], dma=True)
                for qt in range(8):
                    def fz1(e, qt=qt):
                        return e.memset(AR2f[:, 0:2048], 0.0)
                    g.op("V", fz1, writes=[(A2F, 0, 2048)])
                    def fst1(e, qt=qt):
                        return e.dma_start(out=out_hbm[qt*128:(qt+1)*128, :],
                                           in_=AR2f[:, 0:2048])
                    g.op("Y", fst1, reads=[(A2F, 0, 2048)],
                         writes=[("outhbm", qt, qt+1)], dma=True)
                g.barrier_wait("G", [(f"G{i}", g.cnt[f"G{i}"]) for i in range(NSLOT)
                                     if g.cnt[f"G{i}"] > 0])
                g.barrier_wait("S", [(f"S{i}", g.cnt[f"S{i}"]) for i in range(NSLOT)
                                     if g.cnt[f"S{i}"] > 0])
            else:
                for hc in range(FC):
                    sg_ = hc % 2
                    def flg(e, hc=hc, sg_=sg_):
                        return e.dma_start(out=AR2b[:, wstage_col(sg_):wstage_col(sg_) + 2048],
                                           in_=d_wgs[hc])
                    g.op("S", flg, reads=[(DWG, hc, hc + 1)],
                         writes=[(A2B, wstage_col(sg_), wstage_col(sg_) + 2048)], dma=True)
                    def flu(e, hc=hc, sg_=sg_):
                        return e.dma_start(out=AR2b[:, wstage_col(2 + sg_):wstage_col(2 + sg_) + 2048],
                                           in_=d_wus[hc])
                    g.op("S", flu, reads=[(DWU, hc, hc + 1)],
                         writes=[(A2B, wstage_col(2 + sg_), wstage_col(2 + sg_) + 2048)], dma=True)
                    for qb in range(QB):
                        bg = qb
                        bu = 2 + qb
                        for k in range(DC):
                            def fmg(e, sg_=sg_, k=k, qb=qb, bg=bg):
                                lhs = AR2b[:, wstage_col(sg_) + k * 128: wstage_col(sg_) + (k + 1) * 128]
                                rhs = XQ[:, k * T + qb * 512: k * T + (qb + 1) * 512]
                                return e.matmul(PS[bg][:], lhs, rhs,
                                                start=(k == 0), stop=(k == DC - 1))
                            g.op("T", fmg,
                                 reads=[(A2B, wstage_col(sg_) + k * 128,
                                         wstage_col(sg_) + (k + 1) * 128),
                                        (AXQ, k * T + qb * 512, k * T + (qb + 1) * 512)],
                                 writes=[bank(bg)])
                        for k in range(DC):
                            def fmu(e, sg_=sg_, k=k, qb=qb, bu=bu):
                                lhs = AR2b[:, wstage_col(2 + sg_) + k * 128:
                                           wstage_col(2 + sg_) + (k + 1) * 128]
                                rhs = XQ[:, k * T + qb * 512: k * T + (qb + 1) * 512]
                                return e.matmul(PS[bu][:], lhs, rhs,
                                                start=(k == 0), stop=(k == DC - 1))
                            g.op("T", fmu,
                                 reads=[(A2B, wstage_col(2 + sg_) + k * 128,
                                         wstage_col(2 + sg_) + (k + 1) * 128),
                                        (AXQ, k * T + qb * 512, k * T + (qb + 1) * 512)],
                                 writes=[bank(bu)])
                        sslot = AR2F_SCR_A + (qb) * 512
                        def fsg(e, bg=bg, sslot=sslot):
                            return e.activation(AR2f[:, sslot:sslot + 512], PS[bg][:], AF.Silu)
                        g.op("S", fsg, reads=[bank(bg)], writes=[(A2F, sslot, sslot + 512)])
                        dst = p_col(hc, qb * 512)
                        def fgate(e, bu=bu, sslot=sslot, dst=dst):
                            return e.tensor_tensor(AR1[:, dst:dst + 512], PS[bu][:],
                                                   AR2f[:, sslot:sslot + 512], ALU.mult)
                        g.op("V", fgate, reads=[bank(bu), (A2F, sslot, sslot + 512)],
                             writes=[(A1, dst, dst + 512)])

                for mb in range(4):
                    for qth in range(2):
                        for hc in range(FC):
                            ws = hc % 2
                            def fld(e, hc=hc, mb=mb, ws=ws):
                                return e.dma_start(
                                    out=AR2b[:, wdslice_col(ws):wdslice_col(ws) + 512],
                                    in_=d_wds[hc, :, mb * 512:(mb + 1) * 512])
                            g.op("S", fld, reads=[(DWD, hc, hc + 1)],
                                 writes=[(A2B, wdslice_col(ws), wdslice_col(ws) + 512)],
                                 dma=True)
                            for qi in range(4):
                                qt = qth * 4 + qi
                                def fmd(e, hc=hc, qt=qt, ws=ws, qi=qi):
                                    lhs = AR1[:, p_col(hc, qt * 128):p_col(hc, qt * 128) + 128]
                                    rhs = AR2b[:, wdslice_col(ws):wdslice_col(ws) + 512]
                                    return e.matmul(PS[qi][:], lhs, rhs,
                                                    start=(hc == 0), stop=(hc == FC - 1))
                                g.op("T", fmd,
                                     reads=[(A1, p_col(hc, qt * 128), p_col(hc, qt * 128) + 128),
                                            (A2B, wdslice_col(ws), wdslice_col(ws) + 512)],
                                     writes=[bank(qi)])
                        for qi in range(4):
                            qt = qth * 4 + qi
                            sslot = AR2F_SCR_A + (qi % 2) * 512
                            def fsl(e, qi=qi, sslot=sslot):
                                return e.activation(AR2f[:, sslot:sslot + 512], PS[qi][:],
                                                    AF.Silu)
                            g.op("S", fsl, reads=[bank(qi)], writes=[(A2F, sslot, sslot + 512)])
                            def fgs(e, qt=qt, mb=mb, sslot=sslot):
                                return e.dma_start(out=d_g[qt, :, mb * 512:(mb + 1) * 512],
                                                   in_=AR2f[:, sslot:sslot + 512])
                            g.op("Y", fgs, reads=[(A2F, sslot, sslot + 512)],
                                 writes=[(DG, qt * 4 + mb, qt * 4 + mb + 1)], dma=True)

                if phases == "p014":
                    if debug:
                        def fdg(e):
                            return e.dma_start(out=dbg["dbg_g"],
                                               in_=d_g.rearrange("t p n -> (t p) n"))
                        g.op("Y", fdg, reads=[(DG, 0, 32)], writes=[("dbgout", 11, 12)], dma=True)
                    for qt in range(8):
                        def fz2(e, qt=qt):
                            return e.memset(AR2f[:, 0:2048], 0.0)
                        g.op("V", fz2, writes=[(A2F, 0, 2048)])
                        def fst2(e, qt=qt):
                            return e.dma_start(out=out_hbm[qt*128:(qt+1)*128, :],
                                               in_=AR2f[:, 0:2048])
                        g.op("Y", fst2, reads=[(A2F, 0, 2048)],
                             writes=[("outhbm", qt, qt+1)], dma=True)
                    g.barrier_wait("G", [(f"G{i}", g.cnt[f"G{i}"]) for i in range(NSLOT)
                                         if g.cnt[f"G{i}"] > 0])
                    g.barrier_wait("S", [(f"S{i}", g.cnt[f"S{i}"]) for i in range(NSLOT)
                                         if g.cnt[f"S{i}"] > 0])
                else:
                    PIECES = {"masks", "preload", "kbuf", "vbuf", "q", "mm"}
                    if p2mode.startswith("pieces:"):
                        PIECES = set(p2mode[7:].split(",")) if p2mode[7:] else set()
                    if "masks" in PIECES:
                        def fml(e):
                            return e.dma_start(out=AR2b[:, MASKS_COL:MASKS_COL + 4096], in_=masks_in)
                        g.op("S", fml, writes=[(A2B, MASKS_COL, MASKS_COL + 4096)], dma=True)

                    MASK_SLOTS = {0: 0, 1: 1, 2: 2, 3: 3, 16: 4, 17: 5, 18: 6, 19: 7}

                    def load_pair(hp, pb):
                        h0 = 2 * hp
                        for hl in (range(2) if "kbuf" in PIECES else []):
                            for tb in range(6):
                                dst = kbuf_col(pb, hl) + tb * 512
                                def fkl(e, h0=h0, hl=hl, tb=tb, dst=dst):
                                    return e.dma_start(out=AR1[:, dst:dst + 512],
                                                       in_=d_kf[h0 + hl, tb])
                                g.op("S", fkl, reads=[(DKF, (h0 + hl) * 6 + tb,
                                                       (h0 + hl) * 6 + tb + 1)],
                                     writes=[(A1, dst, dst + 512)], dma=True)
                        if "vbuf" not in PIECES:
                            return
                        for tt in range(CT):
                            dst = vbuf_col(pb, tt)
                            def fvl(e, h0=h0, tt=tt, dst=dst):
                                return e.dma_start(out=AR1[:, dst:dst + 256],
                                                   in_=d_vtm[tt, :, h0 * 128:(h0 + 2) * 128])
                            g.op("S", fvl, reads=[(DVT, tt * 4, tt * 4 + 4)],
                                 writes=[(A1, dst, dst + 256)], dma=True)

                    def load_q(h, qb, slot):
                        if "q" not in PIECES:
                            return
                        dst = qring_col(slot)
                        def fql(e, h=h, qb=qb, dst=dst):
                            return e.dma_start(out=AR2b[:, dst:dst + 512], in_=d_qf[h, qb])
                        g.op("S", fql, reads=[(DQF, h * 2 + qb, h * 2 + qb + 1)],
                             writes=[(A2B, dst, dst + 512)], dma=True)

                    load_pair(0, 0)
                    load_q(0, 0, 0)
                    qtoggle = 0
                    for hp in range(8):
                        pb = hp % 2
                        if hp < 7:
                            load_pair(hp + 1, (hp + 1) % 2)
                        for hl in range(2):
                            h = 2 * hp + hl
                            for qb in range(QB):
                                qslot = qtoggle
                                nqh, nqb = (h, qb + 1) if qb + 1 < QB else (h + 1, 0)
                                if nqh < H:
                                    load_q(nqh, nqb, qtoggle ^ 1)
                                qtoggle ^= 1
                                cbank = (h * 2 + qb) % 2 + 2
                                for j in range(J):
                                    sb = j % 2
                                    ktloc = qb * 512 + 128 * j
                                    ktt = ktloc // 128
                                    kcol = kbuf_col(pb, hl) + ktloc
                                    if "mm" not in PIECES:
                                        continue
                                    def fms(e, kcol=kcol, qslot=qslot, sb=sb):
                                        return e.matmul(PS[sb][:],
                                                        AR1[:, kcol:kcol + 128],
                                                        AR2b[:, qring_col(qslot):qring_col(qslot) + 512],
                                                        start=True, stop=True)
                                    g.op("T", fms, reads=[(A1, kcol, kcol + 128),
                                                          (A2B, qring_col(qslot),
                                                           qring_col(qslot) + 512)],
                                         writes=[bank(sb)])
                                    ptl = ptil_col(j % 4)
                                    bcol = qb * J + j
                                    def fex(e, sb=sb, ptl=ptl, bcol=bcol):
                                        return e.activation(AR2b[:, ptl:ptl + 512], PS[sb][:],
                                                            AF.Exp, bias=padb[:, bcol:bcol + 1],
                                                            scale=SCALE)
                                    g.op("S", fex, reads=[bank(sb), cst("padb")],
                                         writes=[(A2B, ptl, ptl + 512)])
                                    if j in MASK_SLOTS:
                                        mcol = MASKS_COL + MASK_SLOTS[j] * 512
                                        def fmk(e, ptl=ptl, mcol=mcol):
                                            return e.tensor_tensor(AR2b[:, ptl:ptl + 512],
                                                                   AR2b[:, ptl:ptl + 512],
                                                                   AR2b[:, mcol:mcol + 512],
                                                                   ALU.mult)
                                        g.op("V", fmk, reads=[(A2B, ptl, ptl + 512),
                                                              (A2B, mcol, mcol + 512)],
                                             writes=[(A2B, ptl, ptl + 512)])
                                    vcol = vbuf_col(pb, ktt) + hl * 128
                                    if p2mode != "sonly":
                                        def fav(e, vcol=vcol, ptl=ptl, cbank=cbank, j=j):
                                            return e.matmul(PS[cbank][:],
                                                            AR1[:, vcol:vcol + 128],
                                                            AR2b[:, ptl:ptl + 512],
                                                            start=(j == 0), stop=(j == J - 1))
                                        g.op("T", fav, reads=[(A1, vcol, vcol + 128),
                                                              (A2B, ptl, ptl + 512)],
                                             writes=[bank(cbank)])
                                    if p2mode not in ("nodn", "sonly"):
                                        def fdn(e, ptl=ptl, j=j):
                                            return e.matmul(PS[4][0:1, :], onesc[:],
                                                            AR2b[:, ptl:ptl + 512],
                                                            start=(j == 0), stop=(j == J - 1))
                                        g.op("T", fdn, reads=[cst("onesc"), (A2B, ptl, ptl + 512)],
                                             writes=[bank(4)])
                                if "mm" not in PIECES or p2mode == "sonly":
                                    continue
                                if p2mode == "nodn":
                                    dstq = ctx_col(h, qb * 512)
                                    def fdv0(e, cbank=cbank, dstq=dstq):
                                        return e.tensor_copy(AR1[:, dstq:dstq + 512], PS[cbank][:])
                                    g.op("V", fdv0, reads=[bank(cbank)],
                                         writes=[(A1, dstq, dstq + 512)])
                                    continue
                                g.op("V", lambda e: e.reciprocal(linv[:], PS[4][0:1, :]),
                                     reads=[bank(4)], writes=[cst("linv")])
                                if debug:
                                    hqi = h * 2 + qb
                                    def fdl(e, hqi=hqi):
                                        return e.dma_start(out=dbg["dbg_linv"][hqi:hqi+1, :],
                                                           in_=linv[:])
                                    g.op("Y", fdl, reads=[cst("linv")],
                                         writes=[("dbgout", 20 + hqi, 21 + hqi)], dma=True)
                                def fbc(e):
                                    return e.matmul(PS[5][:], onesr[:], linv[:],
                                                    start=True, stop=True)
                                g.op("T", fbc, reads=[cst("onesr"), cst("linv")],
                                     writes=[bank(5)])
                                g.op("V", lambda e: e.tensor_copy(linvb[:], PS[5][:]),
                                     reads=[bank(5)], writes=[cst("linvb")])
                                dst = ctx_col(h, qb * 512)
                                def fdv(e, cbank=cbank, dst=dst):
                                    return e.tensor_tensor(AR1[:, dst:dst + 512], PS[cbank][:],
                                                           linvb[:], ALU.mult)
                                g.op("V", fdv, reads=[bank(cbank), cst("linvb")],
                                     writes=[(A1, dst, dst + 512)])

                    if debug:
                        for i2 in range(2):
                            def fdc(e, i2=i2):
                                return e.dma_start(out=dbg["dbg_ctx"][:, i2*8192:(i2+1)*8192],
                                                   in_=AR1[:, i2*8192:(i2+1)*8192])
                            g.op("Y", fdc, reads=[(A1, i2*8192, (i2+1)*8192)],
                                 writes=[("dbgout", 2, 3)], dma=True)

                    if phases == "p0142":
                        for qt in range(8):
                            def fz3(e, qt=qt):
                                return e.memset(AR2f[:, 0:2048], 0.0)
                            g.op("V", fz3, writes=[(A2F, 0, 2048)])
                            def fst3(e, qt=qt):
                                return e.dma_start(out=out_hbm[qt*128:(qt+1)*128, :], in_=AR2f[:, 0:2048])
                            g.op("Y", fst3, reads=[(A2F, 0, 2048)], writes=[("outhbm", qt, qt+1)], dma=True)
                        g.barrier_wait("G", [(f"G{i}", g.cnt[f"G{i}"]) for i in range(NSLOT) if g.cnt[f"G{i}"] > 0])
                        g.barrier_wait("S", [(f"S{i}", g.cnt[f"S{i}"]) for i in range(NSLOT) if g.cnt[f"S{i}"] > 0])
                    else:
                        pass
                        for a in range(4):
                            for k in range(DC):
                                dst = wo_col(k) + a * 512
                                def fwo(e, a=a, k=k, dst=dst):
                                    return e.dma_start(out=AR1[:, dst:dst + 512],
                                                       in_=wo[k * 128:(k + 1) * 128,
                                                              a * 512:(a + 1) * 512])
                                g.op("G", fwo, writes=[(A1, dst, dst + 512)], dma=True)

                        for qt in range(8):
                            xlo = xslot_col(0)
                            olo = xslot_col(1)
                            def fx3(e, qt=qt, xlo=xlo):
                                return e.dma_start(out=AR2f[:, xlo:xlo + 2048],
                                                  in_=x_tm[2048 + qt * 128: 2048 + (qt + 1) * 128, :])
                            g.op("S", fx3, writes=[(A2F, xlo, xlo + 2048)], dma=True)
                            glo = 4096
                            def fg3(e, qt=qt, glo=glo):
                                return e.dma_start(out=AR2f[:, glo:glo + 2048], in_=d_g[qt])
                            g.op("S", fg3, reads=[(DG, qt * 4, qt * 4 + 4)],
                                 writes=[(A2F, glo, glo + 2048)], dma=True)
                            for mb in range(4):
                                bk = mb % 2
                                for hh in range(H):
                                    def fwo2(e, hh=hh, qt=qt, mb=mb, bk=bk):
                                        lhs = AR1[:, ctx_col(hh, qt * 128):ctx_col(hh, qt * 128) + 128]
                                        rhs = AR1[:, wo_col(hh) + mb * 512: wo_col(hh) + (mb + 1) * 512]
                                        return e.matmul(PS[bk][:], lhs, rhs,
                                                        start=(hh == 0), stop=(hh == H - 1))
                                    g.op("T", fwo2,
                                         reads=[(A1, ctx_col(hh, qt * 128), ctx_col(hh, qt * 128) + 128),
                                                (A1, wo_col(hh) + mb * 512, wo_col(hh) + (mb + 1) * 512)],
                                         writes=[bank(bk)])
                                def fc1(e, bk=bk, glo=glo, mb=mb, olo=olo):
                                    return e.tensor_tensor(AR2f[:, olo + mb * 512: olo + (mb + 1) * 512],
                                                           PS[bk][:],
                                                           AR2f[:, glo + mb * 512: glo + (mb + 1) * 512],
                                                           ALU.mult)
                                g.op("V", fc1, reads=[bank(bk), (A2F, glo + mb * 512,
                                                                 glo + (mb + 1) * 512)],
                                     writes=[(A2F, olo + mb * 512, olo + (mb + 1) * 512)])
                                def fc2(e, mb=mb, olo=olo, xlo=xlo):
                                    return e.tensor_tensor(AR2f[:, olo + mb * 512: olo + (mb + 1) * 512],
                                                           AR2f[:, olo + mb * 512: olo + (mb + 1) * 512],
                                                           AR2f[:, xlo + mb * 512: xlo + (mb + 1) * 512],
                                                           ALU.add)
                                g.op("V", fc2, reads=[(A2F, olo + mb * 512, olo + (mb + 1) * 512),
                                                      (A2F, xlo + mb * 512, xlo + (mb + 1) * 512)],
                                     writes=[(A2F, olo + mb * 512, olo + (mb + 1) * 512)])
                            def fst(e, qt=qt, olo=olo):
                                return e.dma_start(out=out_hbm[qt * 128:(qt + 1) * 128, :],
                                                   in_=AR2f[:, olo:olo + 2048])
                            g.op("Y", fst, reads=[(A2F, olo, olo + 2048)],
                                 writes=[("outhbm", qt, qt + 1)], dma=True)

        g.barrier_wait("Y", [(f"Y{i}", g.cnt[f"Y{i}"]) for i in range(NSLOT)
                             if g.cnt[f"Y{i}"] > 0])

        semmap = sems
        @block.tensor
        def _(e):
            for th in g.thunks["T"]:
                th(e, semmap)
        @block.vector
        def _(e):
            for th in g.thunks["V"]:
                th(e, semmap)
        @block.scalar
        def _(e):
            for th in g.thunks["S"]:
                th(e, semmap)
        @block.sync
        def _(e):
            for th in g.thunks["Y"]:
                th(e, semmap)
        @block.gpsimd
        def _(e):
            for th in g.thunks["G"]:
                th(e, semmap)

    return nc


_NC_CACHE = None

def _get_nc():
    global _NC_CACHE
    if _NC_CACHE is None:
        _NC_CACHE = build_program()
    return _NC_CACHE


def _host_inputs(L0_attn_x, L0_attn_Wq, L0_attn_Wk, L0_attn_Wv, L0_attn_Wo,
                 L0_ffn_W_g, L0_ffn_W_u, L0_ffn_W_d):
    x = np.ascontiguousarray(np.asarray(L0_attn_x, dtype=np.float32))
    wq = np.ascontiguousarray(np.asarray(L0_attn_Wq, dtype=np.float32))
    wk = np.ascontiguousarray(np.asarray(L0_attn_Wk, dtype=np.float32))
    wv = np.ascontiguousarray(np.asarray(L0_attn_Wv, dtype=np.float32))
    wo = np.ascontiguousarray(np.asarray(L0_attn_Wo, dtype=np.float32))
    wgT = np.ascontiguousarray(np.asarray(L0_ffn_W_g, dtype=np.float32).T)
    wuT = np.ascontiguousarray(np.asarray(L0_ffn_W_u, dtype=np.float32).T)
    wdT = np.ascontiguousarray(np.asarray(L0_ffn_W_d, dtype=np.float32).T)

    p = np.arange(128)[:, None]
    jj = np.arange(512)[None, :]
    masks = np.zeros((128, 8 * 512), dtype=np.float32)
    for j in range(4):
        masks[:, j * 512:(j + 1) * 512] = (jj < p + 128 * j)
    for idx, j in enumerate(range(4)):
        masks[:, (4 + idx) * 512:(5 + idx) * 512] = (jj >= p + 128 * j)
    masks8 = masks.astype(BFNP)

    ident = np.eye(128, dtype=np.float32).astype(BFNP)
    onesc = np.ones((128, 1), dtype=np.float32).astype(BFNP)
    onesr = np.ones((1, 128), dtype=np.float32)

    in_maps = []
    for c in range(NCORES):
        b, i = divmod(c, 4)
        own0 = T * i
        lo = own0 - 2048
        x_tm = np.zeros((C, D), dtype=np.float32)
        src_lo = max(lo, 0)
        x_tm[src_lo - lo: C] = x[b, src_lo: own0 + T]
        padbias = np.zeros((128, 2 * J), dtype=np.float32)
        for qb in range(QB):
            for j in range(J):
                if own0 + 512 * qb - 2048 + 128 * j < 0:
                    padbias[:, qb * J + j] = PAD_BIAS
        in_maps.append({
            "x_tm": x_tm, "wq": wq, "wk": wk, "wv": wv, "wo": wo,
            "wgT": wgT, "wuT": wuT, "wdT": wdT,
            "masks8": masks8, "padbias": padbias,
            "ident": ident, "onesc": onesc, "onesr": onesr,
        })
    return in_maps


def kernel(L0_attn_x, L0_attn_Wq, L0_attn_Wk, L0_attn_Wv, L0_attn_Wo,
           L0_ffn_W_g, L0_ffn_W_u, L0_ffn_W_d):
    in_maps = _host_inputs(L0_attn_x, L0_attn_Wq, L0_attn_Wk, L0_attn_Wv,
                           L0_attn_Wo, L0_ffn_W_g, L0_ffn_W_u, L0_ffn_W_d)
    nc = _get_nc()
    res = run_bass_kernel_spmd(nc, in_maps, core_ids=list(range(NCORES)))
    outs = [np.asarray(r["out"]) for r in res.results]
    full = np.empty((2, 4096, D), dtype=np.float32)
    for c in range(NCORES):
        b, i = divmod(c, 4)
        full[b, T * i: T * (i + 1)] = outs[c]
    return full
